# version 03
# baseline (speedup 1.0000x reference)
import contextlib
import numpy as np
import concourse.bass as bass
import concourse.mybir as mybir
from concourse.bass_utils import run_bass_kernel_spmd

F32 = mybir.dt.float32
BF16 = mybir.dt.bfloat16
UI32 = mybir.dt.uint32
ALU = mybir.AluOpType
ACT = mybir.ActivationFunctionType
AX = mybir.AxisListType

D = 2048
E = 4096
NST = 16
HD = 5632
NEC = E // 128
NDC = D // 128
NHC = HD // 128
HALO = 3
MAGIC = 3 * 2.0**22
W_CNT = float(HD * D)
EPS_Q = 1e-5
EPS_LN = 1e-5
GGRP = 4
NGRP = NEC // GGRP

DEBUG_OUTS = ()


def build(tok=1024, n_cores=8):
    tokh = tok + HALO
    ntt = tok // 128
    CPE = 4
    lc = tok // CPE
    nchunk = NEC * CPE
    nhb = HD // 512
    nsg = 4
    sg_hc = NHC // nsg
    ntc2 = (tok + 511) // 512

    nc = bass.Bass("TRN2", target_bir_lowering=False)

    x_h = nc.dram_tensor("x", [tokh, D], F32, kind="ExternalInput").ap()
    winst_h = nc.dram_tensor("winst", [64, 128, NDC * 128], F32, kind="ExternalInput").ap()
    wxt_h = nc.dram_tensor("wxt", [128, NEC * 160], F32, kind="ExternalInput").ap()
    wdtt_h = nc.dram_tensor("wdtt", [128, E], F32, kind="ExternalInput").ap()
    woutst_h = nc.dram_tensor("woutst", [128, NEC * D], F32, kind="ExternalInput").ap()
    cw_h = nc.dram_tensor("cw", [128, NEC * 4], F32, kind="ExternalInput").ap()
    bias3_h = nc.dram_tensor("bias3", [128, 3 * NEC], F32, kind="ExternalInput").ap()
    alog_h = nc.dram_tensor("alog", [128, NEC * NST], F32, kind="ExternalInput").ap()
    w1sh_h = nc.dram_tensor("w1sh", [D // 8, HD], F32, kind="ExternalInput").ap()
    w3sh_h = nc.dram_tensor("w3sh", [D // 8, HD], F32, kind="ExternalInput").ap()
    w2sh_h = nc.dram_tensor("w2sh", [HD // 8, D], F32, kind="ExternalInput").ap()
    meta_h = nc.dram_tensor("meta", [1, 4], UI32, kind="ExternalInput").ap()
    masks_h = nc.dram_tensor("masks", [128, 2], F32, kind="ExternalInput").ap()
    ident_h = nc.dram_tensor("identh", [128, 128], F32, kind="ExternalInput").ap()
    dum_h = nc.dram_tensor("dum", [1, 4], F32, kind="Internal").ap()
    swd_h = nc.dram_tensor("swd", [1, 8], F32, kind="Internal").ap()
    bcd_h = nc.dram_tensor("bcd", [32, tok], F32, kind="Internal").ap()
    out_h = nc.dram_tensor("out", [tok, D], F32, kind="ExternalOutput").ap()

    xc_hbm = nc.dram_tensor("xc_hbm", [E, tok], BF16, kind="Internal").ap()
    sz_hbm = nc.dram_tensor("sz_hbm", [E, tok], BF16, kind="Internal").ap()
    yg_hbm = nc.dram_tensor("yg_hbm", [E, tok], BF16, kind="Internal").ap()
    xmid_hbm = nc.dram_tensor("xmid_hbm", [tok, D], F32, kind="Internal").ap()
    hmid_hbm = nc.dram_tensor("hmid_hbm", [tok, HD], BF16, kind="Internal").ap()
    hqt_hbm = nc.dram_tensor("hqt_hbm", [ntt, NHC, 128, 128], BF16, kind="Internal").ap()
    ccp_in = nc.dram_tensor("ccp_in", [1, 8], F32, kind="Internal").ap()
    ccp_out = nc.dram_tensor("ccp_out", [1, 8], F32, kind="Internal").ap()
    ccw_in = [nc.dram_tensor("ccw_in0", [D // 8, HD], BF16, kind="Internal").ap(),
              nc.dram_tensor("ccw_in1", [D // 8, HD], BF16, kind="Internal").ap(),
              nc.dram_tensor("ccw_in2", [HD // 8, D], BF16, kind="Internal").ap()]
    ccw_out = [nc.dram_tensor("ccw_out0", [D, HD], BF16, kind="Internal").ap(),
               nc.dram_tensor("ccw_out1", [D, HD], BF16, kind="Internal").ap(),
               nc.dram_tensor("ccw_out2", [HD, D], BF16, kind="Internal").ap()]
    cch_in = nc.dram_tensor("cch_in", [NGRP, 128, GGRP * NST], F32, kind="Internal").ap()
    cch_out = nc.dram_tensor("cch_out", [NGRP, 128, GGRP * NST], F32, kind="Internal").ap()

    dbg = {}
    if "xc" in DEBUG_OUTS:
        dbg["d_xc"] = nc.dram_tensor("d_xc", [E, tok], BF16, kind="ExternalOutput").ap()
    if "sz" in DEBUG_OUTS:
        dbg["d_sz"] = nc.dram_tensor("d_sz", [E, tok], BF16, kind="ExternalOutput").ap()
    if "dtbc" in DEBUG_OUTS:
        dbg["d_dt"] = nc.dram_tensor("d_dt", [128, tok], BF16, kind="ExternalOutput").ap()
        dbg["d_bc"] = nc.dram_tensor("d_bc", [32, tok], F32, kind="ExternalOutput").ap()
    if "yg" in DEBUG_OUTS:
        dbg["d_yg"] = nc.dram_tensor("d_yg", [E, tok], BF16, kind="ExternalOutput").ap()
    if "xmid" in DEBUG_OUTS:
        dbg["d_xmid"] = nc.dram_tensor("d_xmid", [tok, D], F32, kind="ExternalOutput").ap()

    groups_h = ([[i, i + 1] for i in range(0, n_cores, 2)] if n_cores > 1 else [[0]])
    groups_all = [list(range(n_cores))] if n_cores > 1 else [[0]]

    stack = contextlib.ExitStack()
    with stack:
        def sb(st_, name, shp, dt):
            return st_.enter_context(nc.sbuf_tensor(name, shp, dt)).ap()

        meta_sb = sb(stack, "meta_sb", [1, 4], UI32)
        masks_sb = sb(stack, "masks_sb", [128, 2], F32)
        zero64 = sb(stack, "zero64", [128, GGRP * NST], F32)
        consts = sb(stack, "consts", [128, 2], F32)
        cw_sb = sb(stack, "cw_sb", [128, NEC * 4], F32)
        bias3_sb = sb(stack, "bias3_sb", [128, 3 * NEC], F32)
        aneg_sb = sb(stack, "aneg_sb", [128, NEC * NST], F32)
        wxt_sb = sb(stack, "wxt_sb", [128, NEC * 160], BF16)
        wdtt_sb = sb(stack, "wdtt_sb", [128, E], BF16)
        dt_sb = sb(stack, "dt_sb", [128, tok], BF16)
        bc_sb = sb(stack, "bc_sb", [32, tok], F32)
        sw_vec = sb(stack, "sw_vec", [128, 4], F32)
        swi_vec = sb(stack, "swi_vec", [128, 4], F32)
        ones_f = sb(stack, "ones_f", [128, 1], F32)
        ident = sb(stack, "ident", [128, 128], BF16)
        sred = sb(stack, "sred", [128, 8], F32)
        mv_all = sb(stack, "mv_all", [128, 2 * ntt], F32)
        rstd_all = sb(stack, "rstd_all", [128, 2 * ntt], F32)
        mv2_all = sb(stack, "mv2_all", [128, 2 * ntt], F32)
        s1v = sb(stack, "s1v", [128, ntt], F32)
        c1v = sb(stack, "c1v", [128, ntt], F32)
        c3v = sb(stack, "c3v", [128, ntt], F32)
        s2v = sb(stack, "s2v", [128, ntt], F32)
        c2v = sb(stack, "c2v", [128, ntt], F32)
        habs = sb(stack, "habs", [128, ntt + 1], F32)
        stat6 = sb(stack, "stat6", [128, 24], F32)
        small = sb(stack, "small", [128, 8], F32)
        barsc = sb(stack, "barsc", [128, 1], F32)

        ga = contextlib.ExitStack()
        xbf = [sb(ga, f"xbf{i}", [128, D], BF16) for i in range(2)]
        xT = sb(ga, "xT", [128, NDC, tokh], BF16)
        winst = [sb(ga, f"winst{i}", [128, NDC * 128], BF16) for i in range(2)]
        xp_sb = [sb(ga, f"xp{i}", [128, tokh], F32) for i in range(2)]
        cacc = [sb(ga, f"cacc{i}", [128, tok], F32) for i in range(2)]
        xc_sb = [sb(ga, f"xcsb{i}", [128, tok], BF16) for i in range(2)]
        sz_sb = [sb(ga, f"szsb{i}", [128, tok], BF16) for i in range(2)]
        wsh_sb = sb(ga, "wsh_sb", [128, 11264], F32)
        wq_sb = sb(ga, "wq_sb", [128, 11264], BF16)
        alog_sb = sb(ga, "alog_sb", [128, NEC * NST], F32)
        ga.close()

        gb = contextlib.ExitStack()
        b_bc = sb(gb, "b_bc", [128, NST, tok], BF16)
        c_bc = sb(gb, "c_bc", [128, NST, tok], BF16)
        dA = [sb(gb, f"dA{i}", [128, NST * lc], F32) for i in range(2)]
        dbx = sb(gb, "dbx", [128, NST * lc], BF16)
        hsc = sb(gb, "hsc", [128, NST * lc], BF16)
        delta = [sb(gb, f"delta{i}", [128, lc], F32) for i in range(2)]
        sfpa = [sb(gb, f"sfpa{i}", [128, lc], F32) for i in range(2)]
        sfpr = [sb(gb, f"sfpr{i}", [128, lc], F32) for i in range(2)]
        u_sb = sb(gb, "u_sb", [128, lc], BF16)
        ytmp = sb(gb, "ytmp", [128, lc], F32)
        xcs = [sb(gb, f"xcs{i}", [128, tok], BF16) for i in range(2)]
        szs = [sb(gb, f"szs{i}", [128, tok], BF16) for i in range(2)]
        ygst = [sb(gb, f"ygst{i}", [128, lc], BF16) for i in range(2)]
        hcarry = sb(gb, "hcarry", [128, NST], F32)
        hstage = sb(gb, "hstage", [128, GGRP * NST], F32)
        hrecv_all = sb(gb, "hrecv_all", [128, NGRP * GGRP * NST], F32)
        zerocol = sb(gb, "zerocol", [128, 1], F32)
        gb.close()

        gc = contextlib.ExitStack()
        woutst = sb(gc, "woutst_s", [128, NEC * D], BF16)
        ygtt = [sb(gc, f"ygtt{i}", [128, NEC * 128], BF16) for i in range(2)]
        xres = [sb(gc, f"xres{i}", [128, D], F32) for i in range(2)]
        xpre = [sb(gc, f"xpre{i}", [128, D], F32) for i in range(2)]
        gc.close()

        gd1 = contextlib.ExitStack()
        xqT = sb(gd1, "xqT", [128, NDC, tok], BF16)
        xmid_t = [sb(gd1, f"xmidt{i}", [128, D], F32) for i in range(2)]
        xq_t = [sb(gd1, f"xqt{i}", [128, D], BF16) for i in range(2)]
        w1hb = [sb(gd1, f"w1hb{i}", [128, NDC * 512], BF16) for i in range(2)]
        w3hb = [sb(gd1, f"w3hb{i}", [128, NDC * 512], BF16) for i in range(2)]
        y1s = sb(gd1, "y1s", [128, 512], F32)
        y3s = sb(gd1, "y3s", [128, 512], F32)
        hblk = [sb(gd1, f"hblk{i}", [128, 512], BF16) for i in range(2)]
        gd1.close()

        gd1b = contextlib.ExitStack()
        hmid_t = [sb(gd1b, f"hmidt{i}", [128, HD], BF16) for i in range(2)]
        hf32_t = sb(gd1b, "hf32_t", [128, HD], F32)
        hq_t = [sb(gd1b, f"hqt2{i}", [128, HD], BF16) for i in range(2)]
        hqT_t = sb(gd1b, "hqT_t", [128, NHC * 128], BF16)
        gd1b.close()

        gd2 = contextlib.ExitStack()
        outsb = sb(gd2, "outsb", [128, ntt * D], F32)
        w2sg = [sb(gd2, f"w2sg{i}", [128, sg_hc * D], BF16) for i in range(2)]
        hqtsg = [sb(gd2, f"hqtsg{i}", [128, sg_hc * 128], BF16) for i in range(2)]
        xmid_r = [sb(gd2, f"xmidr{i}", [128, D], F32) for i in range(2)]
        gd2.close()

        mm = [stack.enter_context(nc.psum_tensor(f"mm{i}", [128, 512], F32)).ap() for i in range(2)]
        aux = [stack.enter_context(nc.psum_tensor(f"aux{i}", [128, 512], F32)).ap() for i in range(4)]
        pt = stack.enter_context(nc.psum_tensor("pt", [128, 1024], BF16)).ap()

        sem_names = ["ld", "cast", "xte", "mmev", "xpe", "conv", "xc", "spill",
                     "dbc", "dte", "bce", "wdt", "sabs", "sfpd", "vr", "vdel",
                     "dAs", "scank", "ecdone", "scang", "stgv", "stgq", "cc",
                     "hrdy", "wq", "prep", "bar", "wol", "att", "lne", "lns",
                     "sqr", "xmw", "aqs", "aqv", "xqt", "ffn1", "ffe", "ffv",
                     "hqv", "hqs", "hqtw", "hqd", "f2", "oacc", "czi", "href",
                     "xmr0", "xmr1", "ln2", "sqr2",
                     "xb0", "xb1", "wi0", "wi1", "xl0", "xl1", "yg0", "yg1",
                     "wh0", "wh1", "yl0", "yl1", "xr0", "xr1", "xm0", "xm1",
                     "hm0", "hm1", "hs0", "hs1", "w20", "w21", "hq0", "hq1",
                     "xp0", "xp1", "st0", "st1", "st", "prepp", "vs", "swb", "bcb", "xs0", "xs1", "zs0", "zs1", "dcr"]
        S = {nm: stack.enter_context(nc.semaphore(nm)) for nm in sem_names}

        block = stack.enter_context(nc.Block())

        def emit(me, eng):
            V = me == "vector"
            T = me == "tensor"
            SC = me == "scalar"
            P = me == "gpsimd"
            Q = me == "sync"
            c = {nm: 0 for nm in sem_names}

            def inc(instr, name, k=1):
                c[name] += k
                if instr is not None:
                    instr.then_inc(S[name], k)
                return c[name]

            def w(name, val=None):
                eng.wait_ge(S[name], c[name] if val is None else val)

            def barrier():
                c["bar"] += 19
                tgt = c["bar"]
                if V:
                    eng.engine_nop().then_inc(S["bar"], 1)
                elif P:
                    eng.engine_nop().then_inc(S["bar"], 1)
                elif SC:
                    eng.wait_ge(S["prep"], 1)
                    eng.copy(barsc, consts[:, 0:1]).then_inc(S["bar"], 1)
                elif Q:
                    eng.wait_ge(S["prep"], 1)
                    eng.dma_start(out=dum_h[0:1, 0:4], in_=zero64[0:1, 0:4]).then_inc(S["bar"], 16)
                eng.wait_ge(S["bar"], tgt)

            if Q:
                for _src, _dst in [(meta_h, meta_sb), (masks_h, masks_sb), (cw_h, cw_sb),
                                   (bias3_h, bias3_sb), (alog_h, alog_sb)]:
                    i = eng.dma_start(out=_dst, in_=_src)
                    inc(i, "ld", 16)
                i = eng.dma_start(out=wsh_sb,
                                  in_=w1sh_h.rearrange("a h -> (a h)").rearrange("(p j) -> p j", p=128))
                inc(i, "ld", 16)
            else:
                c["ld"] += 6 * 16
            if P:
                i = eng.dma_start(out=ident, in_=ident_h)
                inc(i, "prepp", 16)
                i = eng.dma_start(out=wxt_sb, in_=wxt_h)
                inc(i, "cast", 16)
                i = eng.dma_start(out=wdtt_sb, in_=wdtt_h)
                inc(i, "cast", 16)
            else:
                c["prepp"] += 16
                c["cast"] += 2 * 16

            if V:
                w("ld", 6 * 16)
                eng.memset(sred, 0.0)
                eng.memset(zero64, 0.0)
                eng.memset(consts[:, 0:1], MAGIC)
                eng.memset(consts[:, 1:2], 1.0)
                eng.memset(habs, 0.0)
                eng.memset(ones_f, 1.0)
                i = eng.engine_nop()
                inc(i, "prep", 1)
            else:
                c["prep"] += 1
            if SC:
                w("ld", 6 * 16)
                w("prep", 1)
                i = eng.activation(aneg_sb, alog_sb, ACT.Exp)
                inc(i, "prep", 1)
            else:
                c["prep"] += 1
            if V:
                w("prep")
                i = eng.tensor_scalar_mul(aneg_sb, aneg_sb, -1.0)
                inc(i, "prep", 1)
            else:
                c["prep"] += 1


            if Q:
                w("prep", 1)
                for g in range(NGRP):
                    i = eng.dma_start(out=cch_in[g], in_=zero64)
                    inc(i, "czi", 16)
            else:
                c["czi"] += 16 * NGRP

            ntile_x = (tokh + 127) // 128
            for it in range(ntile_x):
                r0 = it * 128
                rows = min(128, tokh - r0)
                buf = xbf[it % 2]
                xbs = f"xb{it % 2}"
                if P:
                    if it >= 2:
                        w("mmev", (it - 2) * 2 + 2)
                    i = eng.dma_start(out=buf[0:rows, :], in_=x_h[r0 : r0 + rows, :])
                    inc(i, xbs, 16)
                else:
                    c[xbs] += 16
                for half in range(2):
                    if T:
                        w(xbs, c[xbs])
                        if it == 0 and half == 0:
                            w("prepp", 16)
                        if it > 0 or half > 0:
                            w("xte")
                        for j in range(8):
                            dc = half * 8 + j
                            i = eng.transpose(pt[:, j * 128 : j * 128 + rows],
                                              buf[0:rows, dc * 128 : (dc + 1) * 128],
                                              ident[0:rows, 0:rows])
                        inc(i, "mmev", 1)
                    else:
                        c["mmev"] += 1
                    if V:
                        w("mmev")
                        dst = xT[:, half * 8 : half * 8 + 8, r0 : r0 + rows]
                        src = pt.rearrange("p (j r) -> p j r", j=8)[:, :, 0:rows]
                        i = eng.tensor_copy(dst, src)
                        inc(i, "xte", 1)
                    else:
                        c["xte"] += 1

            for i3 in range(3):
                if i3 > 0:
                    if Q:
                        w("wq", c["wq"])
                        src = [w1sh_h, w3sh_h, w2sh_h][i3]
                        i = eng.dma_start(out=wsh_sb,
                                          in_=src.rearrange("a h -> (a h)").rearrange("(p j) -> p j", p=128))
                        inc(i, "ld", 16)
                    else:
                        c["ld"] += 16
                if V:
                    w("ld", c["ld"])
                    i = eng.tensor_reduce(sred[:, i3 : i3 + 1], wsh_sb, AX.X,
                                          ALU.add, apply_absolute_value=True)
                    inc(i, "wq", 1)
                else:
                    c["wq"] += 1
            if T:
                w("wq", c["wq"])
                w("prep", 3)
                i = eng.matmul(aux[0][0:1, 0:8], ones_f, sred, start=True, stop=True)
                inc(i, "wq", 1)
            else:
                c["wq"] += 1
            if V:
                w("wq", c["wq"])
                i = eng.tensor_copy(small[0:1, 0:8], aux[0][0:1, 0:8])
                inc(i, "wq", 1)
            else:
                c["wq"] += 1
            if Q:
                w("wq", c["wq"])
                i = eng.dma_start(out=ccp_in, in_=small[0:1, 0:8])
                inc(i, "wq", 16)
            else:
                c["wq"] += 16
            if P:
                w("wq", c["wq"])
                i = eng.collective_compute("AllReduce", ALU.add, groups_all,
                                           ins=[ccp_in], outs=[ccp_out])
                inc(i, "wq", 1)
            else:
                c["wq"] += 1
            if Q:
                w("wq", c["wq"])
                i = eng.dma_start(out=small[0:1, 0:8], in_=ccp_out)
                inc(i, "wq", 16)
            else:
                c["wq"] += 16
            if V:
                w("wq", c["wq"])
                i = eng.tensor_scalar(small[0:1, 4:8], small[0:1, 0:4], 1.0 / W_CNT,
                                      EPS_Q, ALU.mult, ALU.max)
                inc(i, "vs", 1)
                w("vs")
                i = eng.reciprocal(small[0:1, 0:4], small[0:1, 4:8])
                inc(i, "wq", 1)
            else:
                c["wq"] += 1
            if Q:
                w("wq", c["wq"])
                i = eng.dma_start(out=swd_h, in_=small[0:1, 0:8])
                inc(i, "wq", 16)
            else:
                c["wq"] += 16
            if P:
                w("wq", c["wq"])
                i = eng.dma_start(out=sw_vec, in_=swd_h[0:1, 0:4].broadcast_to([128, 4]))
                inc(i, "swb", 16)
                i = eng.dma_start(out=swi_vec, in_=swd_h[0:1, 4:8].broadcast_to([128, 4]))
                inc(i, "swb", 16)
            else:
                c["swb"] += 32
            for i3 in range(3):
                if Q:
                    w("wq", c["wq"])
                    w("swb", 32)
                    src = [w1sh_h, w3sh_h, w2sh_h][i3]
                    i = eng.dma_start(out=wsh_sb,
                                      in_=src.rearrange("a h -> (a h)").rearrange("(p j) -> p j", p=128))
                    inc(i, "wq", 16)
                else:
                    c["wq"] += 16
                if SC:
                    w("wq", c["wq"])
                    i = eng.activation(wsh_sb, wsh_sb, ACT.Identity, bias=consts[:, 0:1],
                                       scale=sw_vec[:, i3 : i3 + 1])
                    inc(i, "wq", 1)
                else:
                    c["wq"] += 1
                if V:
                    w("wq", c["wq"])
                    eng.tensor_scalar(wsh_sb, wsh_sb, MAGIC, 1.0, ALU.subtract, ALU.min)
                    eng.drain()
                    i = eng.tensor_scalar(wq_sb, wsh_sb, -1.0, None, ALU.max)
                    inc(i, "wq", 1)
                else:
                    c["wq"] += 1
                if Q:
                    w("wq", c["wq"])
                    i = eng.dma_start(out=ccw_in[i3].rearrange("a h -> (a h)").rearrange("(p j) -> p j", p=128),
                                      in_=wq_sb)
                    inc(i, "wq", 16)
                else:
                    c["wq"] += 16
            if P:
                w("wq", c["wq"])
                i = eng.collective_compute("AllGather", ALU.bypass, groups_all,
                                           ins=[ccw_in[0]], outs=[ccw_out[0]])
                inc(i, "cc", 1)
            else:
                c["cc"] += 1

            mmev_after_sc = []
            xpe_after_sc = []
            for sc in range(64):
                is_xp = sc < NEC
                buf = winst[sc % 2]
                wis = f"wi{sc % 2}"
                if P:
                    if sc >= 2:
                        w("mmev", mmev_after_sc[sc - 2])
                    i = eng.dma_start(out=buf, in_=winst_h[sc, :, :])
                    inc(i, wis, 16)
                else:
                    c[wis] += 16
                if is_xp:
                    tcs = [(k0, min(512, tokh - k0)) for k0 in range(0, tokh, 512)]
                else:
                    tcs = [(HALO + k0, min(512, tok - k0)) for k0 in range(0, tok, 512)]
                for itc, (c0, cwd) in enumerate(tcs):
                    pb = mm[c["xpe"] % 2]
                    if T:
                        w(wis, c[wis])
                        if c["xpe"] >= 2:
                            w("xpe", c["xpe"] - 1)
                        for dc in range(NDC):
                            i = eng.matmul(pb[:, 0:cwd], buf[:, dc * 128 : (dc + 1) * 128],
                                           xT[:, dc, c0 : c0 + cwd],
                                           start=(dc == 0), stop=(dc == NDC - 1))
                        inc(i, "mmev", 1)
                    else:
                        c["mmev"] += 1
                    if SC:
                        w("mmev", c["mmev"])
                        if is_xp:
                            i = eng.copy(xp_sb[sc % 2][:, c0 : c0 + cwd], pb[:, 0:cwd])
                        else:
                            if sc >= NEC + 2 and c0 == HALO:
                                w(f"zs{sc % 2}", 16 * ((sc - NEC) // 2))
                            i = eng.activation(sz_sb[sc % 2][:, c0 - HALO : c0 - HALO + cwd],
                                               pb[:, 0:cwd], ACT.Silu)
                        inc(i, "xpe", 1)
                    else:
                        c["xpe"] += 1
                mmev_after_sc.append(c["mmev"])
                xpe_after_sc.append(c["xpe"])

                if is_xp:
                    if V:
                        w("xpe", c["xpe"])
                        if sc >= 2:
                            w("xc", c["xc"])
                        a0 = cacc[sc % 2]
                        eng.tensor_scalar_mul(a0, xp_sb[sc % 2][:, 0:tok],
                                              cw_sb[:, sc * 4 : sc * 4 + 1])
                        for k in range(1, 4):
                            eng.drain()
                            i = eng.scalar_tensor_tensor(
                                a0, xp_sb[sc % 2][:, k : tok + k],
                                cw_sb[:, sc * 4 + k : sc * 4 + k + 1], a0,
                                ALU.mult, ALU.add)
                        inc(i, "conv", 1)
                    else:
                        c["conv"] += 1
                    if SC:
                        w("conv", c["conv"])
                        if sc >= 2:
                            w(f"xs{sc % 2}", 16 * (sc // 2))
                            w("dbc", sc - 1)
                        i = eng.activation(xc_sb[sc % 2], cacc[sc % 2], ACT.Silu,
                                           bias=bias3_sb[:, sc : sc + 1])
                        inc(i, "xc", 1)
                    else:
                        c["xc"] += 1
                    if T:
                        w("xc", c["xc"])
                        if sc == 0:
                            w("cast", 32)
                            w("wq", 2)
                        for itc in range(ntc2):
                            c0 = itc * 512
                            cwd = min(512, tok - c0)
                            eng.matmul(aux[itc][:, 0:cwd],
                                       wxt_sb[:, sc * 160 : sc * 160 + 128],
                                       xc_sb[sc % 2][:, c0 : c0 + cwd],
                                       start=(sc == 0), stop=(sc == NEC - 1))
                            i = eng.matmul(aux[2 + itc][0:32, 0:cwd],
                                           wxt_sb[:, sc * 160 + 128 : sc * 160 + 160],
                                           xc_sb[sc % 2][:, c0 : c0 + cwd],
                                           start=(sc == 0), stop=(sc == NEC - 1))
                        inc(i, "dbc", 1)
                    else:
                        c["dbc"] += 1
                    if Q:
                        w("xc", c["xc"])
                        i = eng.dma_start(out=xc_hbm[sc * 128 : (sc + 1) * 128, :],
                                          in_=xc_sb[sc % 2])
                        inc(i, f"xs{sc % 2}", 16)
                    else:
                        c[f"xs{sc % 2}"] += 16
                else:
                    if Q:
                        w("xpe", xpe_after_sc[sc])
                        i = eng.dma_start(out=sz_hbm[(sc - NEC) * 128 : (sc - NEC + 1) * 128, :],
                                          in_=sz_sb[sc % 2])
                        inc(i, f"zs{sc % 2}", 16)
                    else:
                        c[f"zs{sc % 2}"] += 16

            if SC:
                w("dbc", c["dbc"])
                for itc in range(ntc2):
                    c0 = itc * 512
                    cwd = min(512, tok - c0)
                    i = eng.copy(dt_sb[:, c0 : c0 + cwd], aux[itc][:, 0:cwd])
                inc(i, "dte", 1)
            else:
                c["dte"] += 1
            if V:
                w("dbc", c["dbc"])
                for itc in range(ntc2):
                    c0 = itc * 512
                    cwd = min(512, tok - c0)
                    i = eng.tensor_copy(bc_sb[:, c0 : c0 + cwd], aux[2 + itc][0:32, 0:cwd])
                inc(i, "bce", 1)
            else:
                c["bce"] += 1

            if "dtbc" in DEBUG_OUTS:
                if Q:
                    w("dte", c["dte"])
                    w("bce", c["bce"])
                    i = eng.dma_start(out=dbg["d_dt"], in_=dt_sb)
                    inc(i, "st", 16)
                    i = eng.dma_start(out=dbg["d_bc"], in_=bc_sb)
                    inc(i, "st", 16)
                else:
                    c["st"] += 32
            if "xc" in DEBUG_OUTS:
                if Q:
                    w("xs0", c["xs0"])
                    w("xs1", c["xs1"])
                    i = eng.dma_start(out=dbg["d_xc"], in_=xc_hbm)
                    inc(i, "st", 16)
                else:
                    c["st"] += 16
            if "sz" in DEBUG_OUTS:
                if Q:
                    w("zs0", c["zs0"])
                    w("zs1", c["zs1"])
                    i = eng.dma_start(out=dbg["d_sz"], in_=sz_hbm)
                    inc(i, "st", 16)
                else:
                    c["st"] += 16

            if Q:
                for nm in ("xs0", "xs1", "zs0", "zs1", "wq", "ld"):
                    w(nm, c[nm])
            barrier()

            if Q:
                w("bce", c["bce"])
                i = eng.dma_start(out=bcd_h, in_=bc_sb)
                inc(i, "bce", 16)
            else:
                c["bce"] += 16
            if P:
                w("bce", c["bce"])
                i = eng.dma_start(out=b_bc, in_=bcd_h[0:NST, :][None, :, :].broadcast_to([128, NST, tok]))
                inc(i, "bcb", 16)
                i = eng.dma_start(out=c_bc, in_=bcd_h[NST : 2 * NST, :][None, :, :].broadcast_to([128, NST, tok]))
                inc(i, "bcb", 16)
            else:
                c["bcb"] += 32
            if V:
                w("bce", c["bce"])
                eng.memset(zerocol, 0.0)
                eng.memset(hstage, 0.0)
                eng.memset(hrecv_all, 0.0)
                i = eng.engine_nop()
                inc(i, "prep", 1)
            else:
                c["prep"] += 1

            def sfp_emit(k):
                ec_, lcx = divmod(k, CPE)
                pb = mm[k % 2]
                if SC:
                    w("wdt", k + 1)
                    if k >= 2:
                        w("vdel", k - 1)
                    i = eng.activation(sfpa[k % 2], pb[:, 0:lc], ACT.Abs,
                                       bias=bias3_sb[:, NEC + ec_ : NEC + ec_ + 1])
                    inc(i, "sabs", 1)
                    eng.drain()
                    eng.activation(sfpa[k % 2], sfpa[k % 2], ACT.Exp, scale=-1.0)
                    eng.drain()
                    i = eng.activation(sfpa[k % 2], sfpa[k % 2], ACT.Ln, bias=consts[:, 1:2])
                    inc(i, "sfpd", 1)
                else:
                    c["sabs"] += 1
                    c["sfpd"] += 1

            def dA_emit(k):
                ec_, lcx = divmod(k, CPE)
                if SC:
                    w("vdel", k + 1)
                    if k >= 2:
                        w("scank", k - 1)
                    for n in range(NST):
                        i = eng.activation(dA[k % 2][:, n * lc : (n + 1) * lc],
                                           delta[k % 2], ACT.Exp,
                                           scale=aneg_sb[:, ec_ * NST + n : ec_ * NST + n + 1])
                    inc(i, "dAs", 1)
                else:
                    c["dAs"] += 1

            def del_emit(k):
                ec_, lcx = divmod(k, CPE)
                pb = mm[k % 2]
                if V:
                    w("sfpd", k + 1)
                    i = eng.tensor_scalar(sfpr[k % 2], pb[:, 0:lc],
                                          bias3_sb[:, NEC + ec_ : NEC + ec_ + 1], 0.0,
                                          ALU.add, ALU.max)
                    inc(i, "vr", 1)
                    eng.drain()
                    if k >= 2:
                        w("dAs", k - 1)
                    i = eng.tensor_add(delta[k % 2], sfpr[k % 2], sfpa[k % 2])
                    inc(i, "vdel", 1)
                else:
                    c["vr"] += 1
                    c["vdel"] += 1

            def ssm_emit(k, arm=None):
                ec_, lcx = divmod(k, CPE)
                g = ec_ // GGRP
                jg = ec_ % GGRP
                if V:
                    w("dAs", k + 1)
                    w(f"xl{ec_ % 2}", 32 * (ec_ // 2 + 1))
                    w("bcb", 32)
                    eng.tensor_mul(u_sb, delta[k % 2], xcs[ec_ % 2][:, lcx * lc : (lcx + 1) * lc])
                    eng.drain()
                    eng.tensor_mul(dbx.rearrange("p (n l) -> p n l", n=NST),
                                   u_sb[:, None, :].broadcast_to([128, NST, lc]),
                                   b_bc[:, :, lcx * lc : (lcx + 1) * lc])
                    eng.drain()
                    if lcx == 0:
                        if arm == 1 and jg == 0:
                            eng.wait_ge(S["hrdy"], 16 * (g + 1))
                        base = (g * GGRP + jg) * NST
                        for n in range(NST):
                            init_ap = (zerocol[:, 0:1] if arm == 0
                                       else hrecv_all[:, base + n : base + n + 1])
                            eng.tensor_tensor_scan(
                                hsc[:, n * lc : (n + 1) * lc],
                                dA[k % 2][:, n * lc : (n + 1) * lc],
                                dbx[:, n * lc : (n + 1) * lc],
                                init_ap, ALU.mult, ALU.add)
                    else:
                        for n in range(NST):
                            eng.tensor_tensor_scan(
                                hsc[:, n * lc : (n + 1) * lc],
                                dA[k % 2][:, n * lc : (n + 1) * lc],
                                dbx[:, n * lc : (n + 1) * lc],
                                hcarry[:, n : n + 1], ALU.mult, ALU.add)
                    eng.drain()
                    eng.tensor_copy(hcarry,
                                    hsc.rearrange("p (n l) -> p n l", n=NST)[:, :, lc - 1])
                    eng.tensor_mul(dbx.rearrange("p (n l) -> p n l", n=NST),
                                   hsc.rearrange("p (n l) -> p n l", n=NST),
                                   c_bc[:, :, lcx * lc : (lcx + 1) * lc])
                    eng.drain()
                    h8 = NST // 2 * lc
                    eng.tensor_add(dbx[:, 0:h8], dbx[:, 0:h8], dbx[:, h8 : 2 * h8])
                    eng.drain()
                    eng.tensor_add(dbx[:, 0 : h8 // 2], dbx[:, 0 : h8 // 2],
                                   dbx[:, h8 // 2 : h8])
                    eng.drain()
                    eng.tensor_add(dbx[:, 0 : h8 // 4], dbx[:, 0 : h8 // 4],
                                   dbx[:, h8 // 4 : h8 // 2])
                    eng.drain()
                    eng.tensor_add(ytmp, dbx[:, 0:lc], dbx[:, lc : 2 * lc])
                    eng.drain()
                    eng.scalar_tensor_tensor(ytmp, xcs[ec_ % 2][:, lcx * lc : (lcx + 1) * lc],
                                             bias3_sb[:, 2 * NEC + ec_ : 2 * NEC + ec_ + 1],
                                             ytmp, ALU.mult, ALU.add)
                    eng.drain()
                    if k >= 2:
                        w(f"yg{k % 2}", 16 * (k // 2))
                    i = eng.tensor_mul(ygst[k % 2], ytmp,
                                       szs[ec_ % 2][:, lcx * lc : (lcx + 1) * lc])
                    if lcx == CPE - 1 and arm == 0:
                        if jg == 0 and g >= 1:
                            eng.wait_ge(S["stgq"], 16 * g)
                        eng.tensor_copy(
                            hstage[:, jg * NST : (jg + 1) * NST],
                            hsc.rearrange("p (n l) -> p n l", n=NST)[:, :, lc - 1])
                        if jg == GGRP - 1:
                            eng.engine_nop().then_inc(S["stgv"], 1)
                    inc(eng.engine_nop(), "scank", 1)
                    if lcx == CPE - 1:
                        inc(eng.engine_nop(), "ecdone", 1)
                        if jg == GGRP - 1:
                            c["stgv"] += 0
                            inc(eng.engine_nop(), "scang", 1)
                else:
                    c["scank"] += 1
                    if lcx == CPE - 1:
                        c["ecdone"] += 1
                        if jg == GGRP - 1:
                            c["stgv"] += 1
                            c["scang"] += 1

            if T:
                w("dte", 1)
                w("cast", 32)
            for k in range(nchunk):
                ec_, lcx = divmod(k, CPE)
                if T:
                    if k >= 2:
                        w("sabs", k - 1)
                        w("vr", k - 1)
                    i = eng.matmul(mm[k % 2][:, 0:lc], wdtt_sb[:, ec_ * 128 : (ec_ + 1) * 128],
                                   dt_sb[:, lcx * lc : (lcx + 1) * lc], start=True, stop=True)
                    inc(i, "wdt", 1)
                else:
                    c["wdt"] += 1

            if Q:
                def _ssm_load(e2):
                    sname = f"xl{e2 % 2}"
                    i = eng.dma_start(out=xcs[e2 % 2], in_=xc_hbm[e2 * 128 : (e2 + 1) * 128, :])
                    inc(i, sname, 16)
                    i = eng.dma_start(out=szs[e2 % 2], in_=sz_hbm[e2 * 128 : (e2 + 1) * 128, :])
                    inc(i, sname, 16)

                def _ssm_qloop(q_arm):
                    _ssm_load(0)
                    _ssm_load(1)
                    for ec_ in range(NEC):
                        for lcx in range(CPE):
                            k = ec_ * CPE + lcx
                            w("scank", k + 1)
                            i = eng.dma_start(out=yg_hbm[ec_ * 128 : (ec_ + 1) * 128,
                                                         lcx * lc : (lcx + 1) * lc],
                                              in_=ygst[k % 2])
                            inc(i, f"yg{k % 2}", 16)
                        if ec_ + 2 < NEC:
                            w("ecdone", ec_ + 1)
                            _ssm_load(ec_ + 2)
                        if q_arm == 0 and ec_ % GGRP == GGRP - 1:
                            g = ec_ // GGRP
                            eng.wait_ge(S["stgv"], g + 1)
                            if g == 0:
                                eng.wait_ge(S["czi"], 16 * NGRP)
                            if g >= 1:
                                eng.wait_ge(S["stgq"], 16 * g)
                            eng.dma_start(out=cch_in[g], in_=hstage).then_inc(S["stgq"], 16)

                for nm in ("xs0", "xs1", "zs0", "zs1"):
                    w(nm, c[nm])
                par_q = eng.value_load(meta_sb[0:1, 0:1])
                snap_q = dict(c)
                final_q = None
                for q_arm in eng.Switch(par_q, 2):
                    c.clear()
                    c.update(snap_q)
                    if q_arm == 1:
                        eng.wait_ge(S["czi"], 16 * NGRP)
                        for g in range(NGRP):
                            if g >= 1:
                                eng.wait_ge(S["stgq"], 16 * g)
                            eng.dma_start(out=cch_in[g], in_=zero64).then_inc(S["stgq"], 16)
                    _ssm_qloop(q_arm)
                    if final_q is None:
                        final_q = dict(c)
                    else:
                        assert final_q == c, "q arm mismatch"
            else:
                for ec_ in range(NEC):
                    c[f"xl{ec_ % 2}"] += 32
                    for lcx in range(CPE):
                        c[f"yg{(ec_ * CPE + lcx) % 2}"] += 16

            if P:
                w("czi", 16 * NGRP)
                w("ld", 6 * 16)
                for g in range(NGRP):
                    eng.wait_ge(S["stgq"], 16 * (g + 1))
                    eng.collective_compute("AllReduce", ALU.add, groups_h,
                                           ins=[cch_in[g]], outs=[cch_out[g]]
                                           ).then_inc(S["cc"], 1)
                    eng.wait_ge(S["cc"], 2 + g)
                    if g == 0:
                        eng.wait_ge(S["prep"], c["prep"])
                    if g >= 1:
                        eng.wait_ge(S["hrdy"], 16 * g)
                    i = eng.dma_start(
                        out=hrecv_all[:, g * GGRP * NST : (g + 1) * GGRP * NST],
                        in_=cch_out[g])
                    i.then_inc(S["hrdy"], 16)
                c["cc"] += NGRP
                i = eng.collective_compute("AllGather", ALU.bypass, groups_all,
                                           ins=[ccw_in[1]], outs=[ccw_out[1]])
                inc(i, "cc", 1)
                i = eng.collective_compute("AllGather", ALU.bypass, groups_all,
                                           ins=[ccw_in[2]], outs=[ccw_out[2]])
                inc(i, "cc", 1)
            else:
                c["cc"] += NGRP + 2
            c["hrdy"] += 0

            if SC or not (V or T or Q or P):
                pass
            if SC:
                sfp_emit(0)
                for k in range(nchunk):
                    if k + 1 < nchunk:
                        sfp_emit(k + 1)
                    dA_emit(k)

            else:
                for k in range(nchunk):
                    c["sabs"] += 1
                    c["sfpd"] += 1
                    c["dAs"] += 1
            if V:
                par_v = eng.value_load(meta_sb[0:1, 0:1])
                snap = dict(c)
                final_c = None
                for arm in eng.Switch(par_v, 2):
                    c.clear()
                    c.update(snap)
                    if arm == 1:
                        eng.engine_nop().then_inc(S["stgv"], NGRP)
                    del_emit(0)
                    if nchunk > 1:
                        del_emit(1)
                    for k in range(nchunk):
                        ssm_emit(k, arm)
                        if k + 2 < nchunk:
                            del_emit(k + 2)
                    if final_c is None:
                        final_c = dict(c)
                    else:
                        assert final_c == c, "switch arm counter mismatch"
            else:
                for k in range(nchunk):
                    c["vr"] += 1
                    c["vdel"] += 1
                    ssm_emit(k)

            if "yg" in DEBUG_OUTS:
                if Q:
                    w("yg0", c["yg0"])
                    w("yg1", c["yg1"])
                    i = eng.dma_start(out=dbg["d_yg"], in_=yg_hbm)
                    inc(i, "st", 16)
                else:
                    c["st"] += 16

            if P:
                eng.wait_ge(S["hrdy"], 16 * NGRP)
                eng.wait_ge(S["stgq"], 16 * NGRP)
                w("yg0", c["yg0"])
                w("yg1", c["yg1"])
            if Q:
                w("yg0", c["yg0"])
                w("yg1", c["yg1"])
                eng.wait_ge(S["stgq"], 16 * NGRP)
            barrier()

            if P:
                for q4 in range(4):
                    if q4 >= 1:
                        w("wol", 16 * q4)
                    i = eng.dma_start(out=woutst[:, q4 * 8 * D : (q4 + 1) * 8 * D],
                                      in_=woutst_h[:, q4 * 8 * D : (q4 + 1) * 8 * D])
                    inc(i, "wol", 16)
            else:
                c["wol"] += 64
            for tt in range(ntt):
                if Q:
                    if tt == 0:
                        w("yg0", c["yg0"])
                        w("yg1", c["yg1"])
                    if tt >= 2:
                        w("att", 4 * (tt - 1))
                    src = yg_hbm.rearrange("(ec p) t -> p ec t", p=128)[:, :, tt * 128 : (tt + 1) * 128]
                    i = eng.dma_start(out=ygtt[tt % 2].rearrange("p (ec t) -> p ec t", ec=NEC), in_=src)
                    inc(i, f"yl{tt % 2}", 16)
                    if tt >= 2:
                        w("lne", 4 * (tt - 1))
                    i = eng.dma_start(out=xres[tt % 2], in_=x_h[HALO + tt * 128 : HALO + (tt + 1) * 128, :])
                    inc(i, f"xr{tt % 2}", 16)
                else:
                    c[f"yl{tt % 2}"] += 16
                    c[f"xr{tt % 2}"] += 16
                for db in range(4):
                    idx = tt * 4 + db
                    pb = mm[idx % 2]
                    if T:
                        w(f"yl{tt % 2}", 16 * (tt // 2 + 1))
                        if idx >= 2:
                            w("lne", idx - 1)
                        for ec_ in range(NEC):
                            if ec_ % 8 == 0:
                                w("wol", 16 * (ec_ // 8 + 1))
                            i = eng.matmul(pb, ygtt[tt % 2][:, ec_ * 128 : (ec_ + 1) * 128],
                                           woutst[:, ec_ * D + db * 512 : ec_ * D + (db + 1) * 512],
                                           start=(ec_ == 0), stop=(ec_ == NEC - 1))
                        inc(i, "att", 1)
                    else:
                        c["att"] += 1
                    if V:
                        w("att", idx + 1)
                        w(f"xr{tt % 2}", 16 * (tt // 2 + 1))
                        if tt >= 2:
                            w(f"xp{tt % 2}", 16 * (tt // 2))
                        i = eng.tensor_add(xpre[tt % 2][:, db * 512 : (db + 1) * 512],
                                           pb, xres[tt % 2][:, db * 512 : (db + 1) * 512])
                        inc(i, "lne", 1)
                    else:
                        c["lne"] += 1
                if V:
                    eng.drain()
                    for q4 in range(D // 512):
                        i = eng.bn_stats(stat6[:, q4 * 6 : (q4 + 1) * 6],
                                         xpre[tt % 2][:, q4 * 512 : (q4 + 1) * 512])
                    eng.drain()
                    i = eng.bn_aggr(mv_all[:, 2 * tt : 2 * tt + 2], stat6)
                    inc(i, "lns", 1)
                else:
                    c["lns"] += 1
                if Q:
                    w("lns", tt + 1)
                    i = eng.dma_start(out=xmid_hbm[tt * 128 : (tt + 1) * 128, :],
                                      in_=xpre[tt % 2])
                    inc(i, f"xp{tt % 2}", 16)
                else:
                    c[f"xp{tt % 2}"] += 16
            if V:
                w("lns", c["lns"])
                eng.drain()
                i = eng.tensor_scalar_add(small[:, 0:1], mv_all[:, 1 : 2 * ntt : 2], EPS_LN) \
                    if False else eng.tensor_scalar_add(
                        rstd_all[:, 0:ntt],
                        mv_all.rearrange("p (t two) -> p t two", two=2)[:, :, 1], EPS_LN)
                inc(i, "sqr", 1)
            else:
                c["sqr"] += 1
            if SC:
                w("sqr", c["sqr"])
                i = eng.activation(rstd_all[:, ntt : 2 * ntt], rstd_all[:, 0:ntt], ACT.Sqrt)
                inc(i, "sqr", 1)
            else:
                c["sqr"] += 1
            if V:
                w("sqr", c["sqr"])
                i = eng.reciprocal(rstd_all[:, 0:ntt], rstd_all[:, ntt : 2 * ntt])
                inc(i, "sqr", 1)
            else:
                c["sqr"] += 1

            if Q:
                w("xp0", c["xp0"])
                w("xp1", c["xp1"])
            barrier()

            for tt in range(ntt):
                if Q:
                    if tt == 0:
                        w("xp0", c["xp0"])
                        w("xp1", c["xp1"])
                    if tt >= 2:
                        w("aqv", 3 * (tt - 1))
                    i = eng.dma_start(out=xmid_t[tt % 2], in_=xmid_hbm[tt * 128 : (tt + 1) * 128, :])
                    inc(i, f"xm{tt % 2}", 16)
                else:
                    c[f"xm{tt % 2}"] += 16
                if V:
                    if tt == 0:
                        w("swb", 32)
                    w(f"xm{tt % 2}", 16 * (tt // 2 + 1))
                    i = eng.tensor_scalar(xmid_t[tt % 2], xmid_t[tt % 2],
                                          mv_all[:, 2 * tt : 2 * tt + 1],
                                          rstd_all[:, tt : tt + 1],
                                          ALU.subtract, ALU.mult)
                    inc(i, "aqv", 1)
                    eng.drain()
                    eng.tensor_reduce(small[:, 0:1], xmid_t[tt % 2], AX.X, ALU.max,
                                      apply_absolute_value=True)
                    eng.drain()
                    i = eng.tensor_scalar_max(small[:, 0:1], small[:, 0:1], EPS_Q)
                    inc(i, "vs", 1)
                    w("vs")
                    eng.reciprocal(small[:, 1:2], small[:, 0:1])
                    eng.drain()
                    eng.tensor_scalar_mul(s1v[:, tt : tt + 1], small[:, 1:2], 127.0)
                    eng.scalar_tensor_tensor(c1v[:, tt : tt + 1], small[:, 0:1],
                                             1.0 / 127.0, swi_vec[:, 0:1],
                                             ALU.mult, ALU.mult)
                    i = eng.scalar_tensor_tensor(c3v[:, tt : tt + 1], small[:, 0:1],
                                                 1.0 / 127.0, swi_vec[:, 1:2],
                                                 ALU.mult, ALU.mult)
                    inc(i, "aqv", 1)
                else:
                    c["aqv"] += 2
                if Q:
                    w("aqv", c["aqv"])
                    i = eng.dma_start(out=xmid_hbm[tt * 128 : (tt + 1) * 128, :],
                                      in_=xmid_t[tt % 2])
                    inc(i, "xmw", 16)
                else:
                    c["xmw"] += 16
                if SC:
                    w("aqv", c["aqv"])
                    w("xmw", 16 * (tt + 1))
                    i = eng.activation(xmid_t[tt % 2], xmid_t[tt % 2], ACT.Identity,
                                       bias=consts[:, 0:1], scale=s1v[:, tt : tt + 1])
                    inc(i, "aqs", 1)
                else:
                    c["aqs"] += 1
                if V:
                    w("aqs", c["aqs"])
                    w("xmw", 16 * (tt + 1))
                    eng.tensor_scalar(xmid_t[tt % 2], xmid_t[tt % 2], MAGIC, 127.0,
                                      ALU.subtract, ALU.min)
                    eng.drain()
                    i = eng.tensor_scalar(xq_t[tt % 2], xmid_t[tt % 2], -128.0, None,
                                          ALU.max)
                    inc(i, "aqv", 1)
                else:
                    c["aqv"] += 1
                for half in range(2):
                    if T:
                        w("aqv", c["aqv"])
                        if tt > 0 or half > 0:
                            w("xqt", c["xqt"])
                        for j in range(8):
                            dc = half * 8 + j
                            i = eng.transpose(pt[:, j * 128 : (j + 1) * 128],
                                              xq_t[tt % 2][:, dc * 128 : (dc + 1) * 128],
                                              ident)
                        inc(i, "mmev", 1)
                    else:
                        c["mmev"] += 1
                    if V:
                        w("mmev", c["mmev"])
                        dst = xqT[:, half * 8 : half * 8 + 8, tt * 128 : (tt + 1) * 128]
                        src = pt.rearrange("p (j r) -> p j r", j=8)
                        i = eng.tensor_copy(dst, src)
                        inc(i, "xqt", 1)
                    else:
                        c["xqt"] += 1

            for hb in range(nhb):
                if Q:
                    if hb == 0:
                        w("cc", c["cc"] - 1)
                    if hb >= 2:
                        w("ffn1", (hb - 1) * ntt)
                    src1 = ccw_out[0].rearrange("(dc p) h -> p dc h", p=128)[:, :, hb * 512 : (hb + 1) * 512]
                    i = eng.dma_start(out=w1hb[hb % 2].rearrange("p (dc h) -> p dc h", dc=NDC), in_=src1)
                    inc(i, f"wh{hb % 2}", 16)
                    src3 = ccw_out[1].rearrange("(dc p) h -> p dc h", p=128)[:, :, hb * 512 : (hb + 1) * 512]
                    i = eng.dma_start(out=w3hb[hb % 2].rearrange("p (dc h) -> p dc h", dc=NDC), in_=src3)
                    inc(i, f"wh{hb % 2}", 16)
                else:
                    c[f"wh{hb % 2}"] += 32
                for tt in range(ntt):
                    idx = hb * ntt + tt
                    if T:
                        w(f"wh{hb % 2}", 32 * (hb // 2 + 1))
                        w("xqt", 2 * ntt)
                        if idx >= 1:
                            w("ffv", idx)
                            w("ffe", idx)
                        for dc in range(NDC):
                            eng.matmul(mm[0], xqT[:, dc, tt * 128 : (tt + 1) * 128],
                                       w1hb[hb % 2][:, dc * 512 : (dc + 1) * 512],
                                       start=(dc == 0), stop=(dc == NDC - 1))
                            i = eng.matmul(mm[1], xqT[:, dc, tt * 128 : (tt + 1) * 128],
                                           w3hb[hb % 2][:, dc * 512 : (dc + 1) * 512],
                                           start=(dc == 0), stop=(dc == NDC - 1))
                        inc(i, "ffn1", 1)
                    else:
                        c["ffn1"] += 1
                    if SC:
                        w("ffn1", idx + 1)
                        i = eng.activation(y1s, mm[0], ACT.Silu, scale=c1v[:, tt : tt + 1])
                        inc(i, "ffe", 1)
                    else:
                        c["ffe"] += 1
                    if V:
                        w("ffn1", idx + 1)
                        i = eng.tensor_scalar_mul(y3s, mm[1], c3v[:, tt : tt + 1])
                        eng.drain()
                        w("ffe", idx + 1)
                        if idx >= 2:
                            w(f"hs{idx % 2}", 16 * (idx // 2))
                        i = eng.tensor_mul(hblk[idx % 2], y1s, y3s)
                        eng.drain()
                        eng.tensor_reduce(small[:, 0:1], hblk[idx % 2], AX.X, ALU.max,
                                          apply_absolute_value=True)
                        eng.drain()
                        if hb == 0:
                            i = eng.tensor_copy(habs[:, tt : tt + 1], small[:, 0:1])
                        else:
                            i = eng.tensor_max(habs[:, tt : tt + 1], habs[:, tt : tt + 1],
                                               small[:, 0:1])
                        inc(i, "ffv", 1)
                    else:
                        c["ffv"] += 1
                    if Q:
                        w("ffv", idx + 1)
                        i = eng.dma_start(out=hmid_hbm[tt * 128 : (tt + 1) * 128,
                                                       hb * 512 : (hb + 1) * 512],
                                          in_=hblk[idx % 2])
                        inc(i, f"hs{idx % 2}", 16)
                    else:
                        c[f"hs{idx % 2}"] += 16
            if V:
                w("ffv", c["ffv"])
                i = eng.tensor_scalar_max(habs[:, 0:ntt], habs[:, 0:ntt], EPS_Q)
                inc(i, "vs", 1)
                w("vs")
                eng.reciprocal(small[:, 0:ntt] if ntt <= 8 else s2v[:, 0:ntt], habs[:, 0:ntt])
                eng.drain()
                rc = small[:, 0:ntt] if ntt <= 8 else s2v[:, 0:ntt]
                eng.tensor_scalar_mul(s2v[:, 0:ntt], rc, 127.0)
                i = eng.scalar_tensor_tensor(c2v[:, 0:ntt], habs[:, 0:ntt], 1.0 / 127.0,
                                             swi_vec[:, 2:3].broadcast_to([128, ntt]),
                                             ALU.mult, ALU.mult)
                inc(i, "hqv", 1)
            else:
                c["hqv"] += 1

            if Q:
                w("hs0", c["hs0"])
                w("hs1", c["hs1"])
                w("xmw", c["xmw"])
            barrier()

            for tt in range(ntt):
                if Q:
                    if tt == 0:
                        w("hs0", c["hs0"])
                        w("hs1", c["hs1"])
                    if tt >= 2:
                        w("hqs", tt - 1)
                    i = eng.dma_start(out=hmid_t[tt % 2], in_=hmid_hbm[tt * 128 : (tt + 1) * 128, :])
                    inc(i, f"hm{tt % 2}", 16)
                else:
                    c[f"hm{tt % 2}"] += 16
                if SC:
                    w(f"hm{tt % 2}", 16 * (tt // 2 + 1))
                    w("hqv", 1 + tt)
                    i = eng.activation(hf32_t, hmid_t[tt % 2], ACT.Identity,
                                       bias=consts[:, 0:1], scale=s2v[:, tt : tt + 1])
                    inc(i, "hqs", 1)
                else:
                    c["hqs"] += 1
                if V:
                    w("hqs", c["hqs"])
                    eng.tensor_scalar(hf32_t, hf32_t, MAGIC, 127.0, ALU.subtract, ALU.min)
                    eng.drain()
                    i = eng.tensor_scalar(hq_t[tt % 2], hf32_t, -128.0, None, ALU.max)
                    inc(i, "hqv", 1)
                else:
                    c["hqv"] += 1
                nht = (NHC + 7) // 8
                for half in range(nht):
                    j0 = half * 8
                    jn = min(8, NHC - j0)
                    if T:
                        w("hqv", c["hqv"])
                        if tt > 0 or half > 0:
                            w("hqtw", c["hqtw"])
                        for j in range(jn):
                            hc = j0 + j
                            i = eng.transpose(pt[:, j * 128 : (j + 1) * 128],
                                              hq_t[tt % 2][:, hc * 128 : (hc + 1) * 128],
                                              ident)
                        inc(i, "mmev", 1)
                    else:
                        c["mmev"] += 1
                    if V:
                        w("mmev", c["mmev"])
                        if half == 0:
                            w("hqd", c["hqd"])
                        dst = hqT_t[:, j0 * 128 : (j0 + jn) * 128]
                        src = pt[:, 0 : jn * 128]
                        i = eng.tensor_copy(dst, src)
                        inc(i, "hqtw", 1)
                    else:
                        c["hqtw"] += 1
                if Q:
                    w("hqtw", c["hqtw"])
                    i = eng.dma_start(out=hqt_hbm[tt].rearrange("hc p r -> p hc r"),
                                      in_=hqT_t.rearrange("p (hc r) -> p hc r", hc=NHC))
                    inc(i, "hqd", 16)
                else:
                    c["hqd"] += 16

            if Q:
                w("hqd", c["hqd"])
                w("hm0", c["hm0"])
                w("hm1", c["hm1"])
            barrier()

            for sg in range(nsg):
                if Q:
                    if sg == 0:
                        w("cc", c["cc"])
                    if sg >= 2:
                        w("f2", (sg - 1) * ntt * 4)
                    src = ccw_out[2].rearrange("(a p) d -> p a d", p=128)[:, sg * sg_hc : (sg + 1) * sg_hc, :]
                    i = eng.dma_start(out=w2sg[sg % 2].rearrange("p (a d) -> p a d", a=sg_hc), in_=src)
                    inc(i, f"w2{sg % 2}", 16)
                else:
                    c[f"w2{sg % 2}"] += 16
                for tt in range(ntt):
                    if Q:
                        idx0 = sg * ntt + tt
                        if idx0 == 0:
                            w("hqd", c["hqd"])
                        if idx0 >= 2:
                            w("f2", (idx0 - 1) * 4)
                        i = eng.dma_start(out=hqtsg[tt % 2].rearrange("p (a r) -> p a r", a=sg_hc),
                                          in_=hqt_hbm[tt, sg * sg_hc : (sg + 1) * sg_hc]
                                          .rearrange("a p r -> p a r"))
                        inc(i, f"hq{idx0 % 2}", 16)
                    else:
                        idx0 = sg * ntt + tt
                        c[f"hq{idx0 % 2}"] += 16
                    for db in range(4):
                        idx = (sg * ntt + tt) * 4 + db
                        pb = mm[idx % 2]
                        if T:
                            w(f"w2{sg % 2}", 16 * (sg // 2 + 1))
                            w(f"hq{(sg * ntt + tt) % 2}", 16 * ((sg * ntt + tt) // 2 + 1))
                            if idx >= 2:
                                w("oacc", idx - 1)
                            for j in range(sg_hc):
                                i = eng.matmul(pb, hqtsg[tt % 2][:, j * 128 : (j + 1) * 128],
                                               w2sg[sg % 2][:, j * D + db * 512 : j * D + (db + 1) * 512],
                                               start=(j == 0), stop=(j == sg_hc - 1))
                            inc(i, "f2", 1)
                        else:
                            c["f2"] += 1
                        if V:
                            w("f2", idx + 1)
                            dst = outsb[:, tt * D + db * 512 : tt * D + (db + 1) * 512]
                            if sg == 0:
                                i = eng.tensor_copy(dst, pb)
                            else:
                                i = eng.tensor_add(dst, dst, pb)
                            inc(i, "oacc", 1)
                        else:
                            c["oacc"] += 1
            ln2_base = c["ln2"]
            for tt in range(ntt):
                if Q:
                    if tt >= 2:
                        w("ln2", ln2_base + tt - 1)
                    i = eng.dma_start(out=xmid_r[tt % 2], in_=xmid_hbm[tt * 128 : (tt + 1) * 128, :])
                    inc(i, f"xmr{tt % 2}", 16)
                else:
                    c[f"xmr{tt % 2}"] += 16
                if V:
                    w("oacc", c["oacc"])
                    w(f"xmr{tt % 2}", 16 * (tt // 2 + 1))
                    sl = outsb[:, tt * D : (tt + 1) * D]
                    i = eng.scalar_tensor_tensor(sl, sl, c2v[:, tt : tt + 1],
                                                 xmid_r[tt % 2], ALU.mult, ALU.add)
                    eng.drain()
                    for q4 in range(D // 512):
                        eng.bn_stats(stat6[:, q4 * 6 : (q4 + 1) * 6],
                                     sl[:, q4 * 512 : (q4 + 1) * 512])
                    eng.drain()
                    i = eng.bn_aggr(mv2_all[:, 2 * tt : 2 * tt + 2], stat6)
                    inc(i, "ln2", 1)
                else:
                    c["ln2"] += 1
            if V:
                w("ln2", c["ln2"])
                eng.drain()
                i = eng.tensor_scalar_add(rstd_all[:, 0:ntt],
                                          mv2_all.rearrange("p (t two) -> p t two", two=2)[:, :, 1],
                                          EPS_LN)
                inc(i, "sqr2", 1)
            else:
                c["sqr2"] += 1
            if SC:
                w("sqr2", c["sqr2"])
                i = eng.activation(rstd_all[:, ntt : 2 * ntt], rstd_all[:, 0:ntt], ACT.Sqrt)
                inc(i, "sqr2", 1)
            else:
                c["sqr2"] += 1
            if V:
                w("sqr2", c["sqr2"])
                i = eng.reciprocal(rstd_all[:, 0:ntt], rstd_all[:, ntt : 2 * ntt])
                inc(i, "sqr2", 1)
            else:
                c["sqr2"] += 1
            for tt in range(ntt):
                if V:
                    w("sqr2", c["sqr2"])
                    if tt == 0:
                        eng.drain()
                    sl2 = outsb[:, tt * D : (tt + 1) * D]
                    i = eng.tensor_scalar(sl2, sl2,
                                          mv2_all[:, 2 * tt : 2 * tt + 1],
                                          rstd_all[:, tt : tt + 1],
                                          ALU.subtract, ALU.mult)
                    inc(i, "ln2", 1)
                else:
                    c["ln2"] += 1
                if Q:
                    w("ln2", c["ln2"])
                    if tt >= 2:
                        w(f"st{tt % 2}", 16 * (tt // 2))
                    i = eng.dma_start(out=out_h[tt * 128 : (tt + 1) * 128, :],
                                      in_=outsb[:, tt * D : (tt + 1) * D])
                    inc(i, f"st{tt % 2}", 16)
                else:
                    c[f"st{tt % 2}"] += 16
            if "xmid" in DEBUG_OUTS:
                if Q:
                    w("xmw", c["xmw"])
                    i = eng.dma_start(out=dbg["d_xmid"], in_=xmid_hbm)
                    inc(i, "st", 16)
                else:
                    c["st"] += 16
            if Q:
                w("st0", c["st0"])
                w("st1", c["st1"])
                w("st", c["st"])

        @block.vector
        def _(v):
            emit("vector", v)

        @block.scalar
        def _(s):
            emit("scalar", s)

        @block.tensor
        def _(t):
            emit("tensor", t)

        @block.gpsimd
        def _(g):
            emit("gpsimd", g)

        @block.sync
        def _(q):
            emit("sync", q)

    return nc


def make_in_maps(inputs, tok=1024, n_cores=8):
    x = np.asarray(inputs["L0_attn_x"], dtype=np.float32).reshape(-1, D)
    W_in = np.asarray(inputs["L0_attn_W_in"], dtype=np.float32)
    conv_w = np.asarray(inputs["L0_attn_conv_w"], dtype=np.float32)
    conv_b = np.asarray(inputs["L0_attn_conv_b"], dtype=np.float32)
    W_x = np.asarray(inputs["L0_attn_W_x"], dtype=np.float32)
    W_dt = np.asarray(inputs["L0_attn_W_dt"], dtype=np.float32)
    b_dt = np.asarray(inputs["L0_attn_b_dt"], dtype=np.float32)
    A_log = np.asarray(inputs["L0_attn_A_log"], dtype=np.float32)
    Dp = np.asarray(inputs["L0_attn_Dp"], dtype=np.float32)
    W_out = np.asarray(inputs["L0_attn_W_out"], dtype=np.float32)
    w1 = np.asarray(inputs["L0_ffn_w1"], dtype=np.float32)
    w2 = np.asarray(inputs["L0_ffn_w2"], dtype=np.float32)
    w3 = np.asarray(inputs["L0_ffn_w3"], dtype=np.float32)

    winst = np.ascontiguousarray(W_in.reshape(64, 128, NDC, 128).transpose(0, 3, 2, 1)
                                 .reshape(64, 128, NDC * 128))
    wxt = np.ascontiguousarray(W_x.T.reshape(NEC, 128, 160).transpose(1, 0, 2).reshape(128, NEC * 160))
    wdtt = np.ascontiguousarray(W_dt.T)
    woutst = np.ascontiguousarray(W_out.T.reshape(NEC, 128, D).transpose(1, 0, 2).reshape(128, NEC * D))
    cw = np.ascontiguousarray(conv_w.reshape(NEC, 128, 4).transpose(1, 0, 2).reshape(128, NEC * 4))
    bias3 = np.ascontiguousarray(
        np.concatenate([conv_b.reshape(NEC, 128).T, b_dt.reshape(NEC, 128).T,
                        Dp.reshape(NEC, 128).T], axis=1))
    alog = np.ascontiguousarray(A_log.reshape(NEC, 128, NST).transpose(1, 0, 2).reshape(128, NEC * NST))
    w1t = np.ascontiguousarray(w1.T)
    w3t = np.ascontiguousarray(w3.T)
    w2t = np.ascontiguousarray(w2.T)

    d8 = D // 8
    h8 = HD // 8
    maps = []
    for cid in range(n_cores):
        half = cid % 2
        t0 = cid * tok
        if half == 0:
            xs = np.concatenate([np.zeros((HALO, D), np.float32), x[t0 : t0 + tok]], axis=0)
        else:
            xs = x[t0 - HALO : t0 + tok]
        sh = cid if n_cores == 8 else 0
        maps.append({
            "x": np.ascontiguousarray(xs),
            "winst": winst, "wxt": wxt, "wdtt": wdtt, "woutst": woutst,
            "cw": cw, "bias3": bias3, "alog": alog,
            "w1sh": np.ascontiguousarray(w1t[sh * d8 : (sh + 1) * d8]),
            "w3sh": np.ascontiguousarray(w3t[sh * d8 : (sh + 1) * d8]),
            "w2sh": np.ascontiguousarray(w2t[sh * h8 : (sh + 1) * h8]),
            "meta": np.array([[half, cid ^ 1, cid, 0]], dtype=np.uint32),
            "identh": np.eye(128, dtype=np.float32),
            "masks": np.ascontiguousarray(
                np.broadcast_to(np.array([1.0 - half, float(half)], np.float32), (128, 2))),
        })
    return maps


def kernel(**inputs):
    nc = build(tok=1024, n_cores=8)
    in_maps = make_in_maps(inputs, tok=1024, n_cores=8)
    res = run_bass_kernel_spmd(nc, in_maps, core_ids=list(range(8)))
    out = np.empty((8192, D), np.float32)
    for cid in range(8):
        out[cid * 1024 : (cid + 1) * 1024] = res.results[cid]["out"]
    return out.reshape(4, 2048, D)
